# version 04
# baseline (speedup 1.0000x reference)
import math
from contextlib import ExitStack

import numpy as np
import ml_dtypes

import concourse.bass as bass
import concourse.mybir as mybir
from concourse.bass_utils import run_bass_kernel_spmd

F32 = mybir.dt.float32
BF16 = mybir.dt.bfloat16
AF = mybir.ActivationFunctionType
ALU = mybir.AluOpType
AX = mybir.AxisListType
BFNP = ml_dtypes.bfloat16

SEQ = 4096
DM = 2048
NH = 8
HD = 128
DL = NH * HD
TOWN = 2048
FFH = 5632
NHB = FFH // HD
CATT = 1.0 / (30.0 * math.sqrt(128.0))
NW = DM * FFH

SYNC, GPS, VEC, SCA, PE = "sync", "gpsimd", "vector", "scalar", "tensor"
ENGINES = [SYNC, GPS, VEC, SCA, PE]

DEBUG = False
DEBUG_TENSORS = set()
EMIT_LN = True
EMIT_QKV = True
EMIT_GAMMA = True
EMIT_THR = True
EMIT_KQ = True
EMIT_V = True
EMIT_LDT = True
EMIT_BETA = True
EMIT_BIASROW = True
EMIT_KQST = True
EMIT_EVAC = True


def build_nc():
    nc = bass.Bass("TRN2", target_bir_lowering=False)

    DI = lambda n, s, d: nc.dram_tensor(n, s, d, kind="ExternalInput")
    DS = lambda n, s, d: nc.dram_tensor(
        n, s, d, kind="ExternalOutput" if (DEBUG and n in DEBUG_TENSORS) else "Internal")

    d_x = DI("x", [SEQ, DM], F32)
    d_xo = DI("xo", [TOWN, DM], F32)
    d_wq = DI("wq", [128, 16384], F32)
    d_wk = DI("wk", [128, 16384], F32)
    d_wv = DI("wv", [128, 16384], F32)
    d_wo = DI("wo", [128, 32768], F32)
    d_bkq = DI("bkq", [128, 16], F32)
    d_bv = DI("bv", [DL], F32)
    d_bo = DI("bo", [DM], F32)
    d_w1 = DI("w1b", [FFH, DM], F32)
    d_w2 = DI("w2b", [FFH, DM], F32)
    d_w3 = DI("w3t", [FFH, DM], F32)
    d_slopes = DI("slopes8", [8, 1], F32)
    d_irow = DI("irow8", [8, SEQ], F32)
    d_mask = DI("mask01", [128, 128], BF16)
    d_ones1b = DI("ones1b", [1, 128], BF16)
    d_ones128 = DI("ones128", [128, 128], BF16)
    d_onescol = DI("onescol", [128, 1], F32)
    d_ones1f = DI("ones1f", [1, 128], F32)

    xn_d = DS("xn_d", [SEQ, DM], BF16)
    xo_d = DS("xo_d", [TOWN, DM], BF16)
    kT_d = DS("kT_d", [DL, SEQ], BF16)
    qT_d = DS("qT_d", [DL, SEQ], BF16)
    v3_d = DS("v3_d", [NH, 128, SEQ], BF16)
    ahi_d = DS("ahi_d", [8, SEQ], BF16)
    alo_d = DS("alo_d", [8, SEQ], BF16)
    bhi_d = DS("bhi_d", [8, SEQ], BF16)
    blo_d = DS("blo_d", [8, SEQ], BF16)
    thr_d = DS("thr_d", [1, 4], F32)
    qw1_d = DS("qw1_d", [FFH, DM], BF16)
    qw2_d = DS("qw2_d", [FFH, DM], BF16)
    qw3_d = DS("qw3_d", [FFH, DM], BF16)
    ctx4_d = DS("ctx4_d", [512, 8192], BF16)
    own_d = DS("own_d", [256, 8192], BF16)
    rp_d = DS("rp_d", [256, 8192], BF16)
    g_d = DS("g_d", [TOWN, DM], BF16)
    cci_d = nc.dram_tensor("cci_d", [256, 8192], BF16, kind="Internal")
    cco_d = nc.dram_tensor("cco_d", [2048, 8192], BF16, kind="Internal")
    out_d = nc.dram_tensor("out", [TOWN, DM], F32, kind="ExternalOutput")

    es = ExitStack()
    SB = lambda n, sh, dt: es.enter_context(nc.sbuf_tensor(n, sh, dt))

    AB = SB("AB", [128, 81920], BF16)
    AFA = SB("AFA", [128, 10240], F32)
    ST = SB("ST", [128, 64], F32)
    s_bkq = SB("s_bkq", [128, 16], F32)
    s_mask = SB("s_mask", [128, 128], BF16)
    s_ones1b = SB("s_ones1b", [1, 128], BF16)
    s_ones128 = SB("s_ones128", [128, 128], BF16)
    s_onescol = SB("s_onescol", [128, 1], F32)
    s_ones1f = SB("s_ones1f", [1, 128], F32)
    s_bo = SB("s_bo", [1, DM], BF16)
    s_bv = SB("s_bv", [1, DL], BF16)
    s_slopes = SB("s_slopes", [8, 1], F32)
    s_thr = SB("s_thr", [128, 8], F32)
    s_trow = SB("s_trow", [1, 4], F32)
    s_eps = SB("s_eps", [128, 1], F32)
    PS = es.enter_context(nc.psum_tensor("PS", [128, 4096], F32))

    plan = {e: [] for e in ENGINES}
    semcnt = {}
    SEM = {}
    PIDC = {}

    def E(eng, fn=None, waits=(), incs=()):
        plan[eng].append((fn, tuple(w for w in waits if w[1] > 0), tuple(incs)))

    def bump(sem, amt=1):
        semcnt[sem] = semcnt.get(sem, 0) + amt
        return (sem, amt)

    def track(sem, amt):
        semcnt[sem] = semcnt.get(sem, 0) + amt
        return semcnt[sem]

    def cnt(sem):
        return semcnt.get(sem, 0)

    bar_n = [0]

    def barrier():
        bar_n[0] += 1
        n = bar_n[0]
        for e in ENGINES:
            E(e, None, incs=(bump("bar_" + e),))
        for e in ENGINES:
            E(e, None, waits=tuple(("bar_" + x, n) for x in ENGINES))

    def const_loads(sy):
        for dst, src in [
            (s_mask[:, :], d_mask[:, :]), (s_ones1b[:, :], d_ones1b[:, :]),
            (s_ones128[:, :], d_ones128[:, :]), (s_onescol[:, :], d_onescol[:, :]),
            (s_ones1f[:, :], d_ones1f[:, :]), (s_slopes[:, :], d_slopes[:, :]),
        ]:
            sy.dma_start(out=dst, in_=src).then_inc(SEM["ldc"], 16)
        return None

    E(SYNC, const_loads)
    track("ldc", 6 * 16)

    def bias_loads(gp):
        gp.dma_start(out=s_bkq[:, :], in_=d_bkq[:, :]).then_inc(SEM["ldc"], 16)
        if EMIT_BIASROW:
            gp.dma_start(out=s_bv[0:1, :], in_=d_bv[:]).then_inc(SEM["ldc"], 16)
            gp.dma_start(out=s_bo[0:1, :], in_=d_bo[:]).then_inc(SEM["ldc"], 16)
        gp.dma_start(out=AFA[0:8, 0:4096], in_=d_irow[:, :]).then_inc(SEM["ldc"], 16)
        return None

    E(GPS, bias_loads)
    LDC_ALL = track("ldc", (4 if EMIT_BIASROW else 2) * 16)

    ve_eps0 = [0]

    def epsinit(v):
        return v.memset(s_eps[:, :], 1e-5)
    E(VEC, epsinit, incs=(bump("ve"),))
    ve_eps0[0] = cnt("ve")

    bb0 = AFA[0:8, 0:4096]
    bb1 = AFA[0:8, 4096:8192]
    sAhi = AB[0:8, 73728:77824]
    sAlo = AB[0:8, 77824:81920]
    sBhi = AB[0:8, 65536:69632]
    sBlo = AB[0:8, 69632:73728]

    _bsteps = [
        lambda v: v.tensor_scalar(bb0, bb0, s_slopes[0:8, 0:1], None, ALU.mult),
        lambda v: v.tensor_copy(sAhi, bb0),
        lambda v: v.tensor_copy(bb1, sAhi),
        lambda v: v.tensor_tensor(bb0, bb0, bb1, ALU.subtract),
        lambda v: v.tensor_copy(sAlo, bb0),
        lambda v: v.tensor_scalar(sBhi, sAhi, -1.0, None, ALU.mult),
        lambda v: v.tensor_scalar(sBlo, sAlo, -1.0, None, ALU.mult),
    ]
    for _bs in _bsteps:
        if EMIT_BETA:
            E(VEC, _bs, waits=(("ldc", LDC_ALL), ("ve", cnt("ve"))), incs=(bump("ve"),))
    ve_bb = cnt("ve")

    def beta_st(sy):
        sy.dma_start(out=ahi_d[:, :], in_=sAhi).then_inc(SEM["stb"], 16)
        sy.dma_start(out=alo_d[:, :], in_=sAlo).then_inc(SEM["stb"], 16)
        sy.dma_start(out=bhi_d[:, :], in_=sBhi).then_inc(SEM["stb"], 16)
        sy.dma_start(out=blo_d[:, :], in_=sBlo).then_inc(SEM["stb"], 16)
        return None
    if EMIT_BETA:
        E(SYNC, beta_st, waits=(("ve", ve_bb),))
        STB = track("stb", 64)
    else:
        STB = 0

    def make_ln(src_d, dst_d, xb_fn, sq_ap, stage_fn, st_sem, ldsem):
        ve_app = {}

        def emit(i, extra_waits=()):
            c0 = (i % 2) * 16

            def ldx(sy, i=i):
                sy.dma_start(out=xb_fn(i), in_=src_d[i * 128:(i + 1) * 128, :]
                             ).then_inc(SEM[ldsem], 16)
                return None
            wts = list(extra_waits)
            if i >= 2:
                wts.append(("ve", ve_app[i - 2]))
            E(SYNC, ldx, waits=tuple(wts))
            ldx_n = track(ldsem, 16)

            wts = [(ldsem, ldx_n)]
            if i >= 1:
                wts.append(("ve", ve_red[(st_sem, i - 1)]))

            def sq(sc, i=i):
                return sc.activation(sq_ap, xb_fn(i), AF.Square)
            E(SCA, sq, waits=tuple(wts), incs=(bump("sc"),))
            sc_sq = cnt("sc")

            def red(v, i=i, c0=c0):
                v.reduce_sum(ST[:, c0 + 0:c0 + 1], xb_fn(i), axis=AX.X)
                return v.reduce_sum(ST[:, c0 + 1:c0 + 2], sq_ap, axis=AX.X)
            E(VEC, red, waits=((ldsem, ldx_n), ("sc", sc_sq)), incs=(bump("ve"),))
            ve_red[(st_sem, i)] = cnt("ve")

            def statsA(v, i=i, c0=c0):
                v.tensor_scalar(ST[:, c0 + 2:c0 + 3], ST[:, c0 + 0:c0 + 1], -1.0 / DM, None, ALU.mult)
                return v.tensor_scalar(ST[:, c0 + 3:c0 + 4], ST[:, c0 + 1:c0 + 2], 1.0 / DM, None, ALU.mult)
            E(VEC, statsA, waits=(("ve", ve_red[(st_sem, i)]),), incs=(bump("ve"),))
            ve_sa = cnt("ve")

            def statsB(v, i=i, c0=c0):
                return v.tensor_tensor(ST[:, c0 + 4:c0 + 5], ST[:, c0 + 2:c0 + 3], ST[:, c0 + 2:c0 + 3], ALU.mult)
            E(VEC, statsB, waits=(("ve", ve_sa),), incs=(bump("ve"),))
            ve_sb2 = cnt("ve")

            def statsC(v, i=i, c0=c0):
                return v.tensor_tensor(ST[:, c0 + 5:c0 + 6], ST[:, c0 + 3:c0 + 4], ST[:, c0 + 4:c0 + 5], ALU.subtract)
            E(VEC, statsC, waits=(("ve", ve_sb2),), incs=(bump("ve"),))
            ve_var = cnt("ve")

            def rstdA(sc, i=i, c0=c0):
                return sc.activation(ST[:, c0 + 6:c0 + 7], ST[:, c0 + 5:c0 + 6], AF.Ln,
                                     bias=s_eps[:, 0:1])
            E(SCA, rstdA, waits=(("ve", ve_var), ("ve", ve_eps0[0])), incs=(bump("sc"),))
            sc_ra = cnt("sc")

            def rstdB(sc, i=i, c0=c0):
                return sc.activation(ST[:, c0 + 7:c0 + 8], ST[:, c0 + 6:c0 + 7], AF.Exp, scale=-0.5)
            E(SCA, rstdB, waits=(("sc", sc_ra),), incs=(bump("sc"),))
            sc_rstd = cnt("sc")

            wts = [("sc", sc_rstd)]
            nslots = 2 if stage_fn(0) is not stage_fn(1) else 1
            if i >= nslots:
                wts.append((st_sem, 16 * (i - nslots + 1)))

            def appA(v, i=i, c0=c0):
                return v.tensor_tensor(ST[:, c0 + 8:c0 + 9], ST[:, c0 + 2:c0 + 3], ST[:, c0 + 7:c0 + 8], ALU.mult)
            E(VEC, appA, waits=tuple(wts), incs=(bump("ve"),))
            ve_apa = cnt("ve")

            def appB(v, i=i, c0=c0):
                return v.tensor_scalar(stage_fn(i), xb_fn(i), ST[:, c0 + 7:c0 + 8], ST[:, c0 + 8:c0 + 9],
                                       ALU.mult, ALU.add)
            E(VEC, appB, waits=(("ve", ve_apa),), incs=(bump("ve"),))
            ve_app[i] = cnt("ve")

            def stx(sy, i=i):
                sy.dma_start(out=dst_d[i * 128:(i + 1) * 128, :], in_=stage_fn(i)
                             ).then_inc(SEM[st_sem], 16)
                return None
            E(SYNC, stx, waits=(("ve", ve_app[i]),))
            track(st_sem, 16)
        return emit

    ve_red = {}

    xbuf = lambda i: AFA[:, (i % 2) * 2048:(i % 2) * 2048 + 2048]
    sqbuf = AFA[:, 4096:6144]
    xnsb = lambda i: AB[:, 65536 + (i % 2) * 2048: 65536 + (i % 2) * 2048 + 2048]
    wq_sb = AB[:, 0:16384]
    wk_sb = AB[:, 16384:32768]
    wv_sb = AB[:, 32768:49152]
    xnt = lambda tc: AB[:, 49152 + (tc % 2) * 8192: 49152 + (tc % 2) * 8192 + 8192]
    evb = lambda u: AB[:, 69632 + (u % 4) * 512: 69632 + (u % 4) * 512 + 512]

    emit_ln1 = make_ln(d_x, xn_d, xbuf, sqbuf, xnsb, "stxn", "ldx")

    def ldwkqv(gp):
        gp.dma_start(out=wq_sb, in_=d_wq[:, :]).then_inc(SEM["ldw"], 16)
        gp.dma_start(out=wk_sb, in_=d_wk[:, :]).then_inc(SEM["ldw"], 16)
        gp.dma_start(out=wv_sb, in_=d_wv[:, :]).then_inc(SEM["ldw"], 16)
        return None
    E(GPS, ldwkqv)
    LDW_KQV = track("ldw", 48)

    gam_tiles = [(dw, mi, hb) for dw, mi in [(d_w1, 0), (d_w2, 1), (d_w3, 2)]
                 for hb in range(NHB)]
    qf = lambda j: AFA[:, 6144 + (j % 2) * 2048: 6144 + (j % 2) * 2048 + 2048]
    ve_gam = {}
    g_ptr = [0]

    def ginit(v):
        return v.memset(ST[:, 32:35], 0.0)
    E(VEC, ginit, incs=(bump("ve"),))
    ve_gacc0 = cnt("ve")

    def emit_gamma(n):
        if not EMIT_GAMMA:
            return
        for _ in range(n):
            j = g_ptr[0]
            if j >= len(gam_tiles):
                return
            g_ptr[0] += 1
            dw, mi, hb = gam_tiles[j]
            wts = [("stb", STB)] if j < 2 else [("ve", ve_gam[j - 2])]

            def gld(gp, dw=dw, hb=hb, j=j):
                gp.dma_start(out=qf(j), in_=dw[hb * 128:(hb + 1) * 128, :]
                             ).then_inc(SEM["ldg"], 16)
                return None
            E(GPS, gld, waits=tuple(wts))
            gld_n = track("ldg", 16)

            def gred(v, j=j):
                return v.reduce_sum(ST[:, 35 + (j % 2):36 + (j % 2)], qf(j), axis=AX.X,
                                    apply_absolute_value=True)
            E(VEC, gred, waits=(("ldg", gld_n), ("ve", ve_gacc0)), incs=(bump("ve"),))
            ve_r = cnt("ve")

            def gacc(v, j=j, mi=mi):
                return v.tensor_tensor(ST[:, 32 + mi:33 + mi], ST[:, 32 + mi:33 + mi],
                                       ST[:, 35 + (j % 2):36 + (j % 2)], ALU.add)
            E(VEC, gacc, waits=(("ve", ve_r),), incs=(bump("ve"),))
            ve_gam[j] = cnt("ve")

    def pslot(u):
        return PS[:, (u % 8) * 512:(u % 8) * 512 + 512]

    unit = [0]
    ve_evac = {}
    st_evac = {}
    pe_tc_done = {}
    for tc in range(8):
        for i in range(4 * tc, 4 * tc + 4):
            if EMIT_LN:
                emit_ln1(i, extra_waits=(("stb", STB),) if i < 2 else ())
        if not EMIT_QKV:
            continue

        wts = [("stxn", 16 * (4 * tc + 4) if EMIT_LN else 0), ("stb", STB)]
        if tc >= 2:
            wts.append(("pe", pe_tc_done[tc - 2]))
        for db in range(16 if EMIT_LDT else 0):
            def ldt(sy, tc=tc, db=db):
                sy.dma_start(out=xnt(tc)[:, db * 512:(db + 1) * 512],
                             in_=xn_d[tc * 512:(tc + 1) * 512, db * 128:(db + 1) * 128],
                             transpose=True).then_inc(SEM["ldt"], 16)
                return None
            E(SYNC, ldt, waits=tuple(wts) if db == 0 else ())
            track("ldt", 16)
        ldt_n = cnt("ldt") if EMIT_LDT else 0

        units = []
        if EMIT_KQ:
            units += [("K", h) for h in range(NH)] + [("Q", h) for h in range(NH)]
        if EMIT_V:
            units += [("V", (tb, dc)) for tb in range(4) for dc in range(2)]
        for kind, arg in units:
            u = unit[0]
            unit[0] += 1
            wts = [("ldt", ldt_n), ("ldw", LDW_KQV), ("ldc", LDC_ALL)]
            if u >= 8:
                wts.append(("ve", ve_evac[u - 8]))
            if kind in ("K", "Q"):
                h = arg
                wsb = wk_sb if kind == "K" else wq_sb
                bcol = h if kind == "K" else 8 + h

                def mmk(t, tc=tc, h=h, u=u, wsb=wsb):
                    ps = pslot(u)
                    last = None
                    for db in range(16):
                        last = t.matmul(ps, wsb[:, db * 1024 + h * 128: db * 1024 + h * 128 + 128],
                                        xnt(tc)[:, db * 512:(db + 1) * 512],
                                        start=(db == 0), stop=(db == 15))
                    return last
                E(PE, mmk, waits=tuple(wts), incs=(bump("pe"),))
                pe_u = cnt("pe")

                wts2 = [("pe", pe_u)]
                if u >= 4:
                    wts2.append(("stkqv", st_evac[u - 4]))

                def evk(v, u=u, bcol=bcol):
                    return v.tensor_scalar(evb(u), pslot(u), s_bkq[:, bcol:bcol + 1],
                                           None, ALU.add)
                def evknop(v):
                    return v.nop()
                E(VEC, evk if EMIT_EVAC else evknop, waits=tuple(wts2), incs=(bump("ve"),))
                ve_evac[u] = cnt("ve")

                dst = kT_d if kind == "K" else qT_d

                def spk(sc, u=u, h=h, tc=tc, dst=dst):
                    if EMIT_KQST:
                        sc.dma_start(out=dst[h * 128:(h + 1) * 128, tc * 512:(tc + 1) * 512],
                                     in_=evb(u)).then_inc(SEM["stkqv"], 16)
                    else:
                        sc.dma_start(out=kT_d[0:128, 0:512], in_=evb(u)).then_inc(SEM["stkqv"], 16)
                    return None
                E(SCA, spk, waits=(("ve", ve_evac[u]),))
                st_evac[u] = track("stkqv", 16)
            else:
                tb, dc = arg

                def mmv(t, tc=tc, tb=tb, dc=dc, u=u):
                    ps = pslot(u)
                    for db in range(16):
                        t.matmul(ps, xnt(tc)[:, db * 512 + tb * 128: db * 512 + tb * 128 + 128],
                                 wv_sb[:, db * 1024 + dc * 512: db * 1024 + dc * 512 + 512],
                                 start=(db == 0), stop=False)
                    return t.matmul(ps, s_ones1b[0:1, :], s_bv[0:1, dc * 512: dc * 512 + 512],
                                    start=False, stop=True)
                E(PE, mmv, waits=tuple(wts), incs=(bump("pe"),))
                pe_u = cnt("pe")

                wts2 = [("pe", pe_u)]
                if u >= 4:
                    wts2.append(("stkqv", st_evac[u - 4]))

                def evv(v, u=u):
                    return v.tensor_copy(evb(u), pslot(u))
                E(VEC, evv, waits=tuple(wts2), incs=(bump("ve"),))
                ve_evac[u] = cnt("ve")

                kb = tc * 4 + tb

                def spv(sc, u=u, kb=kb, dc=dc):
                    for hh in range(4):
                        sc.dma_start(out=v3_d[4 * dc + hh, :, kb * 128:(kb + 1) * 128],
                                     in_=evb(u)[:, hh * 128:(hh + 1) * 128]
                                     ).then_inc(SEM["stkqv"], 16)
                    return None
                E(SCA, spv, waits=(("ve", ve_evac[u]),))
                st_evac[u] = track("stkqv", 64)
            emit_gamma(1)
        pe_tc_done[tc] = cnt("pe")

    STKQV_ALL = cnt("stkqv")
    emit_gamma(len(gam_tiles))
    ve_gam_all = cnt("ve")

    ve_all_now = cnt("ve")

    if EMIT_THR:
        def thrm(t):
            return t.matmul(PS[0:3, 3584:3585], ST[:, 32:35], s_onescol[:, 0:1],
                            start=True, stop=True)
        E(PE, thrm, waits=(("ve", ve_all_now),), incs=(bump("pe"),))
        pe_thr = cnt("pe")

        def thrv1(v):
            return v.tensor_copy(ST[0:3, 40:41], PS[0:3, 3584:3585])
        E(VEC, thrv1, waits=(("pe", pe_thr),), incs=(bump("ve"),))
        ve_t1 = cnt("ve")

        def thrv2(v):
            return v.tensor_scalar(ST[0:3, 41:42], ST[0:3, 40:41], 0.5 / NW, 0.5e-8,
                                   ALU.mult, ALU.add)
        E(VEC, thrv2, waits=(("ve", ve_t1),), incs=(bump("ve"),))
        ve_thr = cnt("ve")

        def thr_st(sy):
            sy.dma_start(out=thr_d[0:1, 0:3].rearrange("a m -> m a"), in_=ST[0:3, 41:42],
                         allow_slow_non_contiguous=True).then_inc(SEM["stth"], 16)
            sy.wait_ge(SEM["stth"], 16)
            sy.dma_start(out=s_trow[0:1, 0:3], in_=thr_d[0:1, 0:3]).then_inc(SEM["stth"], 16)
            return None
        E(SYNC, thr_st, waits=(("ve", ve_thr),))
        STTH = track("stth", 32)

        def thrb(t):
            return t.matmul(PS[0:128, 3584:3587], s_ones1f[:, :], s_trow[0:1, 0:3],
                            start=True, stop=True)
        E(PE, thrb, waits=(("stth", STTH),), incs=(bump("pe"),))
        pe_thr2 = cnt("pe")

        def thrc1(v):
            return v.tensor_copy(s_thr[:, 0:3], PS[0:128, 3584:3587])
        E(VEC, thrc1, waits=(("pe", pe_thr2),), incs=(bump("ve"),))
        ve_tc1 = cnt("ve")

        def thrc2(v):
            return v.tensor_scalar(s_thr[:, 3:6], s_thr[:, 0:3], -1.0, None, ALU.mult)
        E(VEC, thrc2, waits=(("ve", ve_tc1),), incs=(bump("ve"),))

    barrier()

    k_sb = lambda s: AB[:, (s % 2) * 4096:(s % 2) * 4096 + 4096]
    q_sb = lambda s: AB[:, 8192 + (s % 2) * 4096: 8192 + (s % 2) * 4096 + 4096]
    v_sb = lambda s: AB[:, 16384 + (s % 2) * 4096: 16384 + (s % 2) * 4096 + 4096]
    e_sb = lambda s: AB[:, 24576 + (s % 2) * 2048: 24576 + (s % 2) * 2048 + 2048]
    cstg = lambda s: AB[:, 28672 + (s % 2) * 1024: 28672 + (s % 2) * 1024 + 1024]
    qob = lambda j: AB[:, 30720 + (j % 2) * 2048: 30720 + (j % 2) * 2048 + 2048]
    alp4 = lambda s: AB[0:4, 34816:38912] if s % 2 == 0 else AB[64:68, 34816:38912]
    bet4 = lambda s: AB[0:4, 38912:43008] if s % 2 == 0 else AB[64:68, 38912:43008]
    t_sb = AFA[:, 0:2048]
    rsb = AFA[:, 2048:3072]
    qtmp = AFA[:, 3072:5120]
    SC2 = PS[:, 0:2048]
    CTX = PS[:, 2048:3072]
    DEN = PS[:, 3072:4096]

    qdst = [qw1_d, qw2_d, qw3_d]
    ve_qnt = {}
    st_qnt = {}
    q_ptr = [0]

    def emit_quant(n):
        for _ in range(n):
            j = q_ptr[0]
            if j >= len(gam_tiles):
                return
            q_ptr[0] += 1
            dw, mi, hb = gam_tiles[j]
            wts = []
            if j >= 2:
                wts.append(("ve", ve_qnt[j - 2]))

            def qld(gp, dw=dw, hb=hb, j=j):
                gp.dma_start(out=qf(j), in_=dw[hb * 128:(hb + 1) * 128, :]
                             ).then_inc(SEM["ldq2"], 16)
                return None
            E(GPS, qld, waits=tuple(wts))
            qld_n = track("ldq2", 16)

            wts = [("ldq2", qld_n)]
            if j >= 2:
                wts.append(("stqw", st_qnt[j - 2]))

            def qv1(v, j=j, mi=mi):
                return v.tensor_scalar(qtmp, qf(j), s_thr[:, 3 + mi:4 + mi], None, ALU.is_le)
            E(VEC, qv1, waits=tuple(wts), incs=(bump("ve"),))
            ve_q1 = cnt("ve")

            def qv2(v, j=j, mi=mi):
                return v.scalar_tensor_tensor(qob(j), qf(j), s_thr[:, mi:mi + 1],
                                              qtmp, ALU.is_ge, ALU.subtract)
            E(VEC, qv2, waits=(("ve", ve_q1),), incs=(bump("ve"),))
            ve_qnt[j] = cnt("ve")

            def qst(gp, j=j, mi=mi, hb=hb):
                gp.dma_start(out=qdst[mi][hb * 128:(hb + 1) * 128, :], in_=qob(j)
                             ).then_inc(SEM["stqw"], 16)
                return None
            E(GPS, qst, waits=(("ve", ve_qnt[j]),))
            st_qnt[j] = track("stqw", 16)

    def memset_ab(v):
        v.memset(alp4(0), 1.0)
        v.memset(bet4(0), 1.0)
        v.memset(alp4(1), 1.0)
        return v.memset(bet4(1), 1.0)
    E(VEC, memset_ab, incs=(bump("ve"),))
    ve_ms = cnt("ve")

    def subranges(d0):
        if d0 < 512:
            return [(d0, 512), (512, 1024)]
        return [(d0, 1024)]

    pe_head_last = {}
    prev_tanh = [0]
    ve_evq_prev = [0]
    pe_cd_pair = {}
    cst_i = [0]
    for h in range(NH):
        wts = [("stkqv", STKQV_ALL), ("stb", STB), ("ve", ve_ms)]
        if h >= 2:
            wts.append(("pe", pe_head_last[h - 2]))

        def ldh(sy, h=h):
            sy.dma_start(out=k_sb(h), in_=kT_d[h * 128:(h + 1) * 128, :]).then_inc(SEM["ldh"], 16)
            sy.dma_start(out=q_sb(h), in_=qT_d[h * 128:(h + 1) * 128, :]).then_inc(SEM["ldh"], 16)
            sy.dma_start(out=v_sb(h), in_=v3_d[h, :, :]).then_inc(SEM["ldh"], 16)
            sy.dma_start(out=alp4(h)[2:3, :], in_=ahi_d[h:h + 1, :]).then_inc(SEM["ldh"], 16)
            sy.dma_start(out=alp4(h)[3:4, :], in_=alo_d[h:h + 1, :]).then_inc(SEM["ldh"], 16)
            sy.dma_start(out=bet4(h)[0:1, :], in_=bhi_d[h:h + 1, :]).then_inc(SEM["ldh"], 16)
            sy.dma_start(out=bet4(h)[1:2, :], in_=blo_d[h:h + 1, :]).then_inc(SEM["ldh"], 16)
            return None
        E(SYNC, ldh, waits=tuple(wts))
        ldh_n = track("ldh", 7 * 16)

        for qc in range(4):
            q0 = qc * 1024
            nkb = 8 * (qc + 1)
            npu = nkb // 2
            diag0 = nkb - 8
            sc_exp = {}
            ve_mask = {}

            def delta_of(kb, diag0=diag0, q0=q0):
                return (kb * 128 - q0) if kb >= diag0 else 0

            def emit_smm(kb, h=h, qc=qc, q0=q0, diag0=diag0, ldh_n=ldh_n):
                d0 = delta_of(kb)
                slot = (kb % 2) * 1024
                wts = [("ldh", ldh_n)]
                if prev_tanh[0] > 0:
                    wts.append(("sc", prev_tanh[0]))

                def smm(t, h=h, kb=kb, q0=q0, d0=d0, slot=slot):
                    last = None
                    for (a, b) in subranges(d0):
                        t.matmul(SC2[:, slot + a: slot + b],
                                 k_sb(h)[:, kb * 128:(kb + 1) * 128],
                                 q_sb(h)[:, q0 + a: q0 + b],
                                 start=True, stop=False)
                        last = t.matmul(SC2[:, slot + a: slot + b],
                                        alp4(h)[:, kb * 128:(kb + 1) * 128],
                                        bet4(h)[:, q0 + a: q0 + b],
                                        start=False, stop=True)
                    return last
                E(PE, smm, waits=tuple(wts), incs=(bump("pe"),))
                return cnt("pe")

            def emit_cd(kb, first, last, h=h, qc=qc, diag0=diag0):
                d0 = delta_of(kb)
                slot = (kb % 2) * 1024
                pu = kb // 2
                wts = [("sc", sc_exp[pu])]
                mk = ve_mask.get(kb)
                if mk:
                    wts.append(("ve", mk))
                if first and ve_evq_prev[0] > 0:
                    wts.append(("ve", ve_evq_prev[0]))

                def cd(t, h=h, kb=kb, d0=d0, slot=slot, first=first, last=last, pu=pu):
                    li = None
                    for (a, b) in subranges(d0):
                        t.matmul(CTX[:, a:b], v_sb(h)[:, kb * 128:(kb + 1) * 128],
                                 e_sb(pu)[:, slot + a: slot + b],
                                 start=first, stop=last, skip_group_check=True)
                        li = t.matmul(DEN[:, a:b], s_ones128[:, :],
                                      e_sb(pu)[:, slot + a: slot + b],
                                      start=first, stop=last, skip_group_check=True)
                    return li
                E(PE, cd, waits=tuple(wts), incs=(bump("pe"),))
                return cnt("pe")

            for pu in range(npu):
                kb0, kb1 = 2 * pu, 2 * pu + 1
                emit_smm(kb0)
                pe_smm = emit_smm(kb1)

                d0 = delta_of(kb0)

                def tnh(sc, d0=d0):
                    return sc.activation(t_sb[:, d0:2048], SC2[:, d0:2048], AF.Tanh,
                                         scale=CATT)
                E(SCA, tnh, waits=(("pe", pe_smm),), incs=(bump("sc"),))
                sc_tanh = cnt("sc")
                prev_tanh[0] = sc_tanh

                wts = [("sc", sc_tanh)]
                pcd = pe_cd_pair.get(pu % 2)
                if pcd:
                    wts.append(("pe", pcd))

                def exq(sc, pu=pu, d0=d0):
                    return sc.activation(e_sb(pu)[:, d0:2048], t_sb[:, d0:2048], AF.Exp,
                                         scale=30.0)
                E(SCA, exq, waits=tuple(wts), incs=(bump("sc"),))
                sc_exp[pu] = cnt("sc")

                for kb in (kb0, kb1):
                    if kb >= diag0:
                        dd = delta_of(kb)
                        slot = (kb % 2) * 1024

                        def msk(v, pu=pu, slot=slot, dd=dd):
                            return v.tensor_tensor(e_sb(pu)[:, slot + dd: slot + dd + 128],
                                                   e_sb(pu)[:, slot + dd: slot + dd + 128],
                                                   s_mask[:, :], ALU.mult)
                        E(VEC, msk, waits=(("sc", sc_exp[pu]),), incs=(bump("ve"),))
                        ve_mask[kb] = cnt("ve")

                if pu >= 1:
                    emit_cd(2 * (pu - 1), first=(pu == 1), last=False)
                    pe_cd_pair[(pu - 1) % 2] = emit_cd(2 * pu - 1, first=False, last=False)
                emit_quant(1)

            emit_cd(nkb - 2, first=(npu == 1), last=False)
            pe_last = emit_cd(nkb - 1, first=False, last=True)
            pe_cd_pair[0] = pe_last
            pe_cd_pair[1] = pe_last

            s = cst_i[0]
            wts = [("pe", pe_last)]
            if s >= 2:
                wts.append(("stc", 16 * (s - 1)))

            def evq1(v):
                return v.reciprocal(rsb[:, :], DEN[:, :])
            E(VEC, evq1, waits=tuple(wts), incs=(bump("ve"),))
            ve_rec = cnt("ve")

            def evq2(v, s=s):
                return v.tensor_tensor(cstg(s), CTX[:, :], rsb[:, :], ALU.mult)
            E(VEC, evq2, waits=(("ve", ve_rec),), incs=(bump("ve"),))
            ve_evq_prev[0] = cnt("ve")
            ve_now = cnt("ve")

            def stc(sy, h=h, qc=qc, s=s):
                sy.dma_start(out=ctx4_d[qc * 128:(qc + 1) * 128, h * 1024:(h + 1) * 1024],
                             in_=cstg(s)).then_inc(SEM["stc"], 16)
                return None
            E(SYNC, stc, waits=(("ve", ve_now),))
            track("stc", 16)
            cst_i[0] += 1
        pe_head_last[h] = cnt("pe")
    emit_quant(len(gam_tiles))
    STC_ALL = cnt("stc")
    STQW_ALL = cnt("stqw")

    def ccic(gp):
        pid = gp.partition_id()
        side = pid % 2
        other = (pid + 1) % 2
        gp.dma_start(out=cci_d[:, :], in_=ctx4_d[bass.ds(other * 256, 256), :]
                     ).then_inc(SEM["stcc"], 16)
        gp.dma_start(out=own_d[:, :], in_=ctx4_d[bass.ds(side * 256, 256), :]
                     ).then_inc(SEM["stcc"], 16)
        return None
    E(GPS, ccic, waits=(("stc", STC_ALL),))
    STCC = track("stcc", 32)

    def coll(gp):
        gp.collective_compute(
            "AllGather", ALU.bypass,
            replica_groups=[[0, 1, 2, 3, 4, 5, 6, 7]],
            ins=[cci_d.ap().opt()],
            outs=[cco_d.ap().opt()],
        ).then_inc(SEM["cc"])
        return None
    E(GPS, coll, waits=(("stcc", STCC),))
    track("cc", 1)

    barrier()

    xb3 = lambda i: AFA[:, 3072 + (i % 2) * 2048: 3072 + (i % 2) * 2048 + 2048]
    sq3 = AFA[:, 7168:9216]
    xns3 = lambda i: AB[:, 77824 + (i % 2) * 2048: 77824 + (i % 2) * 2048 + 2048]
    emit_ln3 = make_ln(d_xo, xo_d, xb3, sq3, xns3, "stxn2", "ldx2")
    for i in range(16):
        emit_ln3(i)
    STXN2 = cnt("stxn2")

    xnt3 = lambda db: AB[:, db * 1024: db * 1024 + 1024]
    hid = lambda hb: AB[:, 16384 + hb * 1024: 16384 + hb * 1024 + 1024]
    q1b = lambda s: AB[:, 61440 + (s % 2) * 8192: 61440 + (s % 2) * 8192 + 2048]
    q2b = lambda s: AB[:, 63488 + (s % 2) * 8192: 63488 + (s % 2) * 8192 + 2048]
    panel = lambda s: AB[:, 0:11264] if s % 2 == 0 else AB[:, 61440:72704]
    gstg = lambda s: AB[:, 77824 + (s % 2) * 512: 77824 + (s % 2) * 512 + 256]
    t1b = lambda s: AFA[:, 1024 + (s % 2) * 1024: 1024 + (s % 2) * 1024 + 1024]
    tb3 = lambda s: AFA[:, (s % 2) * 256:(s % 2) * 256 + 256]

    pe_x12 = {}
    sc_t1p = {}
    ve_hid = {}
    pe_fb = {}
    ve_gmul = {}
    st_g = {}
    for half in range(2):
        wts0 = [("stxn2", 16 * (8 * half + 8))]
        if half == 1:
            wts0.append(("pe", pe_fb[(0, 7, 7)]))

        for db in range(16):
            def ldt3(sy, half=half, db=db):
                sy.dma_start(out=xnt3(db),
                             in_=xo_d[half * 1024:(half + 1) * 1024, db * 128:(db + 1) * 128],
                             transpose=True).then_inc(SEM["ldt3"], 16)
                return None
            E(SYNC, ldt3, waits=tuple(wts0) if db == 0 else ())
            track("ldt3", 16)
        ldt3_n = cnt("ldt3")

        for hb in range(NHB):
            wts = [("stqw", STQW_ALL)]
            if hb >= 2:
                wts.append(("pe", pe_x12[(half, hb - 2)]))
            elif half == 1:
                wts.append(("pe", pe_fb[(0, 7, 7)]))

            def ldq12(gp, hb=hb):
                gp.dma_start(out=q1b(hb), in_=qw1_d[hb * 128:(hb + 1) * 128, :]
                             ).then_inc(SEM["ldq12"], 16)
                gp.dma_start(out=q2b(hb), in_=qw2_d[hb * 128:(hb + 1) * 128, :]
                             ).then_inc(SEM["ldq12"], 16)
                return None
            E(GPS, ldq12, waits=tuple(wts))
            ldq12_n = track("ldq12", 32)

            wts = [("ldq12", ldq12_n), ("ldt3", ldt3_n)]
            if hb >= 2:
                wts.append(("ve", ve_hid[(half, hb - 2)]))
            elif half == 1:
                wts.append(("ve", ve_gmul[(0, 7, 7)]))

            def mmx(t, hb=hb):
                pa = PS[:, (hb % 2) * 2048:(hb % 2) * 2048 + 1024]
                pb = PS[:, (hb % 2) * 2048 + 1024:(hb % 2) * 2048 + 2048]
                last = None
                for s in range(2):
                    for db in range(16):
                        t.matmul(pa[:, s * 512:(s + 1) * 512],
                                 q1b(hb)[:, db * 128:(db + 1) * 128],
                                 xnt3(db)[:, s * 512:(s + 1) * 512],
                                 start=(db == 0), stop=(db == 15))
                for s in range(2):
                    for db in range(16):
                        last = t.matmul(pb[:, s * 512:(s + 1) * 512],
                                        q2b(hb)[:, db * 128:(db + 1) * 128],
                                        xnt3(db)[:, s * 512:(s + 1) * 512],
                                        start=(db == 0), stop=(db == 15))
                return last
            E(PE, mmx, waits=tuple(wts), incs=(bump("pe"),))
            pe_x12[(half, hb)] = cnt("pe")

            def sg1(sc, hb=hb):
                return sc.activation(t1b(hb), PS[:, (hb % 2) * 2048:(hb % 2) * 2048 + 1024],
                                     AF.Tanh, scale=0.5)
            E(SCA, sg1, waits=(("pe", pe_x12[(half, hb)]),), incs=(bump("sc"),))
            sc_t1p[(half, hb)] = cnt("sc")

            def hidv1(v, hb=hb):
                return v.tensor_scalar(t1b(hb), t1b(hb), 0.5, 0.5, ALU.mult, ALU.add)
            E(VEC, hidv1, waits=(("sc", sc_t1p[(half, hb)]),), incs=(bump("ve"),))
            ve_h1 = cnt("ve")

            def hidv2(v, hb=hb):
                return v.tensor_tensor(hid(hb), t1b(hb),
                                       PS[:, (hb % 2) * 2048 + 1024:(hb % 2) * 2048 + 2048],
                                       ALU.mult)
            E(VEC, hidv2, waits=(("ve", ve_h1),), incs=(bump("ve"),))
            ve_hid[(half, hb)] = cnt("ve")
        ve_hid_all = cnt("ve")

        useq = [0]
        for s8 in range(8):
            wts = [("ve", ve_hid_all)]
            if s8 >= 2:
                wts.append(("pe", pe_fb[(half, s8 - 2, 7)]))
            else:
                wts.append(("pe", pe_x12[(half, NHB - 1)]))

            def ldpan(gp, half=half, s8=s8):
                for hb0 in range(0, NHB, 4):
                    gp.dma_start(
                        out=panel(s8)[:, hb0 * 256:(hb0 + 4) * 256]
                        .rearrange("p (f m) -> p f m", m=256),
                        in_=qw3_d[hb0 * 128:(hb0 + 4) * 128, s8 * 256:(s8 + 1) * 256]
                        .rearrange("(f p) m -> p f m", p=128),
                    ).then_inc(SEM["ldpan"], 16)
                return None
            E(GPS, ldpan, waits=tuple(wts))
            ldpan_n = track("ldpan", 11 * 16)

            for m in range(8):
                u = useq[0] % 8
                useq[0] += 1
                wts = [("ldpan", ldpan_n)]
                prev = ve_gmul.get((half, s8 - 1, m))
                if prev is None and half == 1:
                    prev = ve_gmul.get((0, 7, m))
                if prev:
                    wts.append(("ve", prev))

                def mmf(t, s8=s8, m=m, u=u):
                    ps = PS[:, u * 512: u * 512 + 256]
                    last = None
                    for hb in range(NHB):
                        last = t.matmul(ps, hid(hb)[:, m * 128:(m + 1) * 128],
                                        panel(s8)[:, hb * 256:(hb + 1) * 256],
                                        start=(hb == 0), stop=(hb == NHB - 1))
                    return last
                E(PE, mmf, waits=tuple(wts), incs=(bump("pe"),))
                pe_fb[(half, s8, m)] = cnt("pe")

                def sil(sc, m=m, u=u):
                    return sc.activation(tb3(m), PS[:, u * 512: u * 512 + 256],
                                         AF.Tanh, scale=0.5)
                E(SCA, sil, waits=(("pe", pe_fb[(half, s8, m)]),), incs=(bump("sc"),))
                sc_sil = cnt("sc")

                wts = [("sc", sc_sil)]
                pg = st_g.get((half, s8, m - 2))
                if pg is None and m < 2:
                    pg = st_g.get((half, s8 - 1, 6 + m))
                    if pg is None and half == 1:
                        pg = st_g.get((0, 7, 6 + m))
                if pg:
                    wts.append(("stg", pg))

                def gmul1(v, m=m):
                    return v.tensor_scalar(tb3(m), tb3(m), 0.5, 0.5, ALU.mult, ALU.add)
                E(VEC, gmul1, waits=tuple(wts), incs=(bump("ve"),))
                ve_g1 = cnt("ve")

                def gmul2(v, m=m, u=u):
                    return v.tensor_tensor(gstg(m), PS[:, u * 512: u * 512 + 256],
                                           tb3(m), ALU.mult)
                E(VEC, gmul2, waits=(("ve", ve_g1),), incs=(bump("ve"),))
                ve_gmul[(half, s8, m)] = cnt("ve")

                def stgf(sy, half=half, s8=s8, m=m):
                    sy.dma_start(out=g_d[half * 1024 + m * 128: half * 1024 + (m + 1) * 128,
                                         s8 * 256:(s8 + 1) * 256],
                                 in_=gstg(m)).then_inc(SEM["stg"], 16)
                    return None
                E(SYNC, stgf, waits=(("ve", ve_gmul[(half, s8, m)]),))
                st_g[(half, s8, m)] = track("stg", 16)

    STG_ALL = cnt("stg")
    barrier()

    ctx4 = AB[:, 0:16384]
    rctx4 = AB[:, 16384:32768]
    wo4 = AB[:, 32768:65536]
    g4 = lambda s: AB[:, 65536 + (s % 2) * 2048: 65536 + (s % 2) * 2048 + 2048]
    tmp4 = AFA[:, 0:2048]
    x4 = lambda s: AFA[:, 2048 + (s % 2) * 2048: 2048 + (s % 2) * 2048 + 2048]
    o4 = lambda s: AFA[:, 6144 + (s % 2) * 2048: 6144 + (s % 2) * 2048 + 2048]

    def p4loads(gp):
        gp.dma_start(out=ctx4.rearrange("p (two m) -> p two m", two=2),
                     in_=own_d[:, :].rearrange("(two p) m -> p two m", p=128)
                     ).then_inc(SEM["ldp4"], 16)
        gp.dma_start(out=wo4, in_=d_wo[:, :]).then_inc(SEM["ldp4"], 16)
        gp.wait_ge(SEM["cc"], 1)
        pid = gp.partition_id()
        pairv = pid + 1 - (pid % 2) * 2
        gp.dma_start(out=rp_d[:, :], in_=cco_d[bass.ds(pairv * 256, 256), :]
                     ).then_inc(SEM["stcc"], 16)
        gp.wait_ge(SEM["stcc"], STCC + 16)
        gp.dma_start(out=rctx4.rearrange("p (two m) -> p two m", two=2),
                     in_=rp_d[:, :].rearrange("(two p) m -> p two m", p=128)
                     ).then_inc(SEM["ldp4"], 16)
        return None
    E(GPS, p4loads, waits=(("stcc", STCC),))
    track("stcc", 16)
    LDP4 = track("ldp4", 48)

    pe_s = {}
    ve_cmb = {}
    st_o = {}
    for m in range(16):
        wts = [("ldp4", LDP4)]
        if m >= 2:
            wts.append(("ve", ve_cmb[m - 2]))

        def mmo(t, m=m):
            ps = PS[:, (m % 2) * 2048:(m % 2) * 2048 + 2048]
            base = (m // 8) * 8192 + (m % 8) * 128
            last = None
            for dc in range(4):
                for g16 in range(16):
                    if g16 < 8:
                        lhsT = ctx4[:, base + g16 * 1024: base + g16 * 1024 + 128]
                    else:
                        lhsT = rctx4[:, base + (g16 - 8) * 1024: base + (g16 - 8) * 1024 + 128]
                    t.matmul(ps[:, dc * 512:(dc + 1) * 512], lhsT,
                             wo4[:, g16 * 2048 + dc * 512: g16 * 2048 + dc * 512 + 512],
                             start=(g16 == 0), stop=False)
                last = t.matmul(ps[:, dc * 512:(dc + 1) * 512], s_ones1b[0:1, :],
                                s_bo[0:1, dc * 512:(dc + 1) * 512], start=False, stop=True)
            return last
        E(PE, mmo, waits=tuple(wts), incs=(bump("pe"),))
        pe_s[m] = cnt("pe")

        wts = [("stg", STG_ALL)]
        if m >= 2:
            wts.append(("ve", ve_cmb[m - 2]))

        def ldg4(gp, m=m):
            gp.dma_start(out=g4(m), in_=g_d[m * 128:(m + 1) * 128, :]).then_inc(SEM["ldg4"], 16)
            return None
        E(GPS, ldg4, waits=tuple(wts))
        ldg4_n = track("ldg4", 16)

        def ldx4(sy, m=m):
            sy.dma_start(out=x4(m), in_=d_xo[m * 128:(m + 1) * 128, :]).then_inc(SEM["ldx4"], 16)
            return None
        E(SYNC, ldx4, waits=tuple(wts))
        ldx4_n = track("ldx4", 16)

        wts = [("pe", pe_s[m]), ("ldg4", ldg4_n), ("ldx4", ldx4_n)]
        if m >= 2:
            wts.append(("sto", st_o[m - 2]))

        def cmb1(v, m=m):
            ps = PS[:, (m % 2) * 2048:(m % 2) * 2048 + 2048]
            return v.tensor_tensor(tmp4, ps, g4(m), ALU.mult)
        E(VEC, cmb1, waits=tuple(wts), incs=(bump("ve"),))
        ve_c1 = cnt("ve")

        def cmb2(v, m=m):
            return v.tensor_tensor(o4(m), tmp4, x4(m), ALU.add)
        E(VEC, cmb2, waits=(("ve", ve_c1),), incs=(bump("ve"),))
        ve_cmb[m] = cnt("ve")

        def sto(gp, m=m):
            gp.dma_start(out=out_d[m * 128:(m + 1) * 128, :], in_=o4(m)).then_inc(SEM["sto"], 16)
            return None
        E(GPS, sto, waits=(("ve", ve_cmb[m]),))
        st_o[m] = track("sto", 16)

    E(GPS, None, waits=(("sto", cnt("sto")),))

    for name in semcnt:
        SEM[name] = es.enter_context(nc.semaphore(name))

    def run_plan(eng, ops):
        for fn, waits, incs in ops:
            for nm, val in waits:
                eng.wait_ge(SEM[nm], val)
            ins = fn(eng) if fn is not None else None
            for nm, amt in incs:
                tgt = ins if ins is not None else eng.nop()
                tgt.then_inc(SEM[nm], amt)

    with nc.Block() as block:
        @block.sync
        def _(e):
            run_plan(e, plan[SYNC])

        @block.gpsimd
        def _(e):
            run_plan(e, plan[GPS])

        @block.vector
        def _(e):
            run_plan(e, plan[VEC])

        @block.scalar
        def _(e):
            run_plan(e, plan[SCA])

        @block.tensor
        def _(e):
            run_plan(e, plan[PE])

    es.close()
    return nc


def _blocked16(WT):
    M = WT.shape[1]
    return np.ascontiguousarray(WT.reshape(16, 128, M).transpose(1, 0, 2).reshape(128, 16 * M))


def _consts():
    jj = np.arange(128)
    mask01 = (jj[None, :] >= jj[:, None]).astype(BFNP)
    return {
        "irow8": np.broadcast_to(np.arange(SEQ, dtype=np.float32), (8, SEQ)).copy(),
        "mask01": mask01,
        "ones1b": np.ones((1, 128), dtype=BFNP),
        "ones128": np.ones((128, 128), dtype=BFNP),
        "onescol": np.ones((128, 1), dtype=np.float32),
        "ones1f": np.ones((1, 128), dtype=np.float32),
    }


def kernel(**inputs):
    x = np.asarray(inputs["L0_attn_hidden_states"], dtype=np.float32)
    Wq = np.asarray(inputs["L0_attn_Wq"], dtype=np.float32)
    bq = np.asarray(inputs["L0_attn_bq"], dtype=np.float32)
    Wk = np.asarray(inputs["L0_attn_Wk"], dtype=np.float32)
    bk = np.asarray(inputs["L0_attn_bk"], dtype=np.float32)
    Wv = np.asarray(inputs["L0_attn_Wv"], dtype=np.float32)
    bv = np.asarray(inputs["L0_attn_bv"], dtype=np.float32)
    Wo = np.asarray(inputs["L0_attn_Wo"], dtype=np.float32)
    bo = np.asarray(inputs["L0_attn_bo"], dtype=np.float32)
    slopes = np.asarray(inputs["L0_attn_alibi_slopes"], dtype=np.float32)
    w1 = np.asarray(inputs["L0_ffn_w1"], dtype=np.float32)
    w2 = np.asarray(inputs["L0_ffn_w2"], dtype=np.float32)
    w3 = np.asarray(inputs["L0_ffn_w3"], dtype=np.float32)

    WqT = np.ascontiguousarray(Wq.T)
    WkT = np.ascontiguousarray(Wk.T)
    WvT = np.ascontiguousarray(Wv.T)
    WoT = np.ascontiguousarray(Wo.T)
    w1b = np.ascontiguousarray(
        w1.T.reshape(16, 128, NHB, 128).transpose(2, 1, 0, 3).reshape(FFH, DM))
    w2b = np.ascontiguousarray(
        w2.T.reshape(16, 128, NHB, 128).transpose(2, 1, 0, 3).reshape(FFH, DM))
    w3t = np.ascontiguousarray(w3.T)

    consts = _consts()
    nc = build_nc()
    in_maps = []
    for c in range(8):
        b = c // 2
        side = c % 2
        d0 = side * DL
        do2 = (1 - side) * DL
        wo_perm = np.concatenate([WoT[d0:d0 + DL], WoT[do2:do2 + DL]], axis=0)
        im = {
            "x": np.ascontiguousarray(x[b]),
            "xo": np.ascontiguousarray(x[b, side * TOWN:(side + 1) * TOWN]),
            "wq": _blocked16(WqT[:, d0:d0 + DL]),
            "wk": _blocked16(WkT[:, d0:d0 + DL]),
            "wv": _blocked16(WvT[:, d0:d0 + DL]),
            "wo": _blocked16(wo_perm),
            "bkq": np.ascontiguousarray(np.concatenate(
                [bk[d0:d0 + DL].reshape(8, 128).T, bq[d0:d0 + DL].reshape(8, 128).T],
                axis=1)),
            "bv": np.ascontiguousarray(bv[d0:d0 + DL]),
            "bo": bo,
            "w1b": w1b, "w2b": w2b, "w3t": w3t,
            "slopes8": np.ascontiguousarray(slopes[side * NH:(side + 1) * NH].reshape(8, 1)),
        }
        im.update(consts)
        in_maps.append(im)

    res = run_bass_kernel_spmd(nc, in_maps, core_ids=list(range(8)))
    out = np.empty((4, SEQ, DM), dtype=np.float32)
    for c in range(8):
        b = c // 2
        side = c % 2
        out[b, side * TOWN:(side + 1) * TOWN, :] = np.asarray(res.results[c]["out"])
    return out.reshape(4, SEQ, DM)
